# version 04
# baseline (speedup 1.0000x reference)
import contextlib
import dataclasses
import os
import time
import numpy as np
import ml_dtypes

import concourse.bass as bass
import concourse.mybir as mybir
from concourse.bass_utils import run_bass_kernel_spmd

F32 = mybir.dt.float32
BF16 = mybir.dt.bfloat16
I32 = mybir.dt.int32
U8 = mybir.dt.uint8
AF = mybir.ActivationFunctionType
ALU = mybir.AluOpType
AX = mybir.AxisListType
nbf = ml_dtypes.bfloat16

D = 1024
DS = 64
TOK = 4096
C = 128
NCH = TOK // C
GRP = 512
NG = TOK // GRP
NPASS = 4
GPP = NG // NPASS
QTOK = TOK // NPASS
TH_ERR = float((1.0 + 1e-6) * np.log(np.float64(np.float32(0.7)) /
                                     np.float64(np.float32(0.3))))
BIG = 100.0
NONCE = 20260826


def _fresh_nonce():
    return ((int(time.time() * 1e6) ^ (os.getpid() << 16)) & 0x7FFFFF00) | 0x1100

ENGINES = ("PE", "DVE", "ACT", "POOL", "SP")


class _Op:
    __slots__ = ("idx", "engine", "emit", "reads", "writes", "is_dma",
                 "deps", "marked", "count", "dma_val", "tag",
                 "self_drain", "drain_val")

    def __init__(self, idx, engine, emit, reads, writes, is_dma, tag=""):
        self.idx = idx; self.engine = engine; self.emit = emit
        self.reads = tuple(reads); self.writes = tuple(writes)
        self.is_dma = is_dma; self.deps = []
        self.marked = False; self.count = None; self.dma_val = None
        self.tag = tag; self.self_drain = 0; self.drain_val = 0


class Sched:

    def __init__(self, nc):
        self.nc = nc
        self.ops = []
        self.last_writer = {}
        self.readers = {}

    def add(self, engine, emit, reads=(), writes=(), is_dma=False, tag="",
            dep_on=()):
        reads = tuple(reads)
        writes = tuple(writes)
        psr = tuple(b for b in reads if b.startswith("ps") and b not in writes)
        if psr:
            writes = writes + psr
            reads = tuple(b for b in reads if not b.startswith("ps"))
        op = _Op(len(self.ops), engine, emit, reads, writes, is_dma, tag)
        deps = {}
        for d in dep_on:
            if d is not None:
                deps[d.idx] = d
        for b in op.reads:
            w = self.last_writer.get(b)
            if w is not None:
                deps[w.idx] = w
        for b in op.writes:
            w = self.last_writer.get(b)
            if w is not None:
                deps[w.idx] = w
            for rr in self.readers.get(b, ()):
                deps[rr.idx] = rr
        deps.pop(op.idx, None)
        op.deps = list(deps.values())
        for b in op.reads:
            self.readers.setdefault(b, []).append(op)
        for b in op.writes:
            self.last_writer[b] = op
            self.readers[b] = []
        self.ops.append(op)
        return op

    def prepare(self):
        issued = {e: 0 for e in ENGINES}
        drains = {e: None for e in ENGINES}
        new_ops = []
        nidx = 10 ** 7
        for op in self.ops:
            ndeps = []
            self_drain = 0
            for d in op.deps:
                if not d.is_dma:
                    ndeps.append(d)
                    continue
                q = d.engine
                if q == op.engine:
                    self_drain = max(self_drain, 16 * issued[q])
                else:
                    dr = drains[q]
                    if dr is None or dr[0] < d.dma_val:
                        dop = _Op(nidx, q, (lambda e: e.nop()), (), (),
                                  False, tag="drain")
                        nidx += 1
                        dop.deps = []
                        dop.drain_val = 16 * issued[q]
                        dop.self_drain = 0
                        new_ops.append(dop)
                        dr = (16 * issued[q], dop)
                        drains[q] = dr
                    ndeps.append(dr[1])
            op.deps = ndeps
            op.self_drain = self_drain
            new_ops.append(op)
            if op.is_dma:
                issued[op.engine] += 1
                op.dma_val = 16 * issued[op.engine]
        self.ops = new_ops
        for op in self.ops:
            for d in op.deps:
                d.marked = True
        counts = {e: 0 for e in ENGINES}
        for op in self.ops:
            if not op.is_dma and op.marked:
                counts[op.engine] += 1
                op.count = counts[op.engine]

    def emit_engine(self, ename, eng, sems, dsems):
        waited = {e: 0 for e in ENGINES}
        dwaited = 0
        for op in self.ops:
            if op.engine != ename:
                continue
            sd = max(getattr(op, "self_drain", 0),
                     getattr(op, "drain_val", 0))
            if sd > dwaited:
                eng.wait_ge(dsems[ename], sd)
                dwaited = sd
            for d in op.deps:
                if d.count is not None and d.count > waited[d.engine]:
                    eng.wait_ge(sems[d.engine], d.count)
                    waited[d.engine] = d.count
            inst = op.emit(eng)
            if op.is_dma:
                inst.then_inc(dsems[ename], 16)
            elif op.marked:
                inst.then_inc(sems[ename], 1)

    def run(self):
        nc = self.nc
        self.prepare()
        stack = contextlib.ExitStack()
        sems = {e: stack.enter_context(nc.semaphore("S" + e)) for e in ENGINES}
        dsems = {e: stack.enter_context(nc.semaphore("D" + e)) for e in ENGINES}
        self._sem_stack = stack
        with nc.Block() as block:
            block.tensor(lambda eng: self.emit_engine("PE", eng, sems, dsems))
            block.vector(lambda eng: self.emit_engine("DVE", eng, sems, dsems))
            block.scalar(lambda eng: self.emit_engine("ACT", eng, sems, dsems))
            block.gpsimd(lambda eng: self.emit_engine("POOL", eng, sems, dsems))
            block.sync(lambda eng: self.emit_engine("SP", eng, sems, dsems))


def _bpar(dram_ap, n=128):
    return dataclasses.replace(dram_ap, ap=[[0, n]] + list(dram_ap.ap[1:]))


def _rep(ap, n):
    return dataclasses.replace(ap, ap=list(ap.ap[:1]) + [[0, n]] + list(ap.ap[1:]))


def _repf(ap, n):
    return dataclasses.replace(ap, ap=list(ap.ap[:1]) + [[0, n]])


def build_attn():
    nc = bass.Bass("TRN2", target_bir_lowering=False, detect_race_conditions=False)
    ctx = contextlib.ExitStack()

    def din(name, shape, dtype=F32):
        return nc.dram_tensor(name, shape, dtype, kind="ExternalInput").ap()

    def dout(name, shape, dtype=F32):
        return nc.dram_tensor(name, shape, dtype, kind="ExternalOutput").ap()

    def sbuf(name, shape, dtype=F32):
        return ctx.enter_context(nc.sbuf_tensor(name, shape, dtype)).ap()

    def psum(name, shape, dtype=F32):
        return ctx.enter_context(nc.psum_tensor(name, shape, dtype)).ap()

    dX = din("xT", [D, TOK])
    dWkT = din("WkT", [D, DS]); dWvT = din("WvT", [D, DS])
    dWqT = din("WqT", [D, DS]); dIR1T = din("ir1T", [D, DS])
    dIR2T = din("ir2T", [DS, 1]); dPWT = din("pwT", [D, 3])
    dNM1T = din("nm1T", [DS, 2 * DS]); dNM2T = din("nm2T", [2 * DS, DS])
    dNMB1 = din("nmb1", [2 * DS, 1]); dNMB2 = din("nmb2", [DS, 1])
    dIRB1 = din("irb1", [DS, 1]); dIRB2 = din("irb2", [1, 1])
    dWoT = din("WoT", [DS, D]); dNW = din("normw", [1, DS])
    dGWT = din("gwT", [D, 8]); dLB = din("lb", [1, 8])
    dFA = din("fa", [1, 1]); dSA = din("sa", [1, 1])
    dID = din("identb", [128, 128], BF16)
    dEX = din("exTS", [128, 128]); dEXT = din("exST", [128, 128])
    dMSL = din("mskSL", [128, 128]); dMSU = din("mskSU", [128, 128])
    dIOR = din("iotaRow", [1, 128]); dIOC = din("iotaCol", [128, 1])
    dIOCR = din("iotaColRev", [128, 1])
    dRELAY = din("relay", [1, 4], I32)
    dISODD = din("isodd", [1, 1])

    oX1 = dout("x1T", [D, TOK])
    oPART = dout("partials", [1, 16])
    oSTATS = dout("stats", [2, TOK])
    oSELS = dout("sels", [8, TOK], BF16)
    _dbgarr = os.environ.get("MK_DBGARR", "")
    oDBG = {}
    if _dbgarr:
        oDBG["dbgD0"] = dout("dbgD0", [128, NCH * 128])
        oDBG["dbgMhT"] = dout("dbgMhT", [DS, NCH * 2 * C], BF16)
        oDBG["dbgVtok"] = dout("dbgVtok", [128, NCH * DS])
        oDBG["dbgQhTf"] = dout("dbgQhTf", [DS, NCH * C], BF16)
        oDBG["dbgNuT"] = dout("dbgNuT", [DS, NCH * C])
        oDBG["dbgDb"] = dout("dbgDb", [128, NCH * 128], BF16)
        oDBG["dbgRbf"] = dout("dbgRbf", [128, NCH * DS], BF16)
        oDBG["dbgNsTok"] = dout("dbgNsTok", [128, NCH * DS])
        oDBG["dbgKpf"] = dout("dbgKpf", [128, NCH * DS], BF16)
        oDBG["dbgPwt"] = dout("dbgPwt", [128, NCH * 4])
    def dSELS_v(g4):
        return dataclasses.replace(
            oSELS, offset=oSELS.offset + g4 * 512,
            ap=[[128, 4], [TOK, 8], [1, 128]])
    dX_v = dX.rearrange("(g p) t -> p g t", p=128)
    dX1_v = oX1.rearrange("(g p) t -> p g t", p=128)

    shState = nc.dram_tensor("shState", [2, 130 * DS], F32, kind="Internal",
                             addr_space="Shared").ap()
    shFlag = nc.dram_tensor("shFlag", [2, 2], I32, kind="Internal",
                            addr_space="Shared").ap()

    sWkT = sbuf("sWkT", [128, 8 * DS], BF16)
    sWvT = sbuf("sWvT", [128, 8 * DS], BF16)
    sWqT = sbuf("sWqT", [128, 8 * DS], BF16)
    sIR1T = sbuf("sIR1T", [128, 8 * DS], BF16)
    sPWT = sbuf("sPWT", [128, 8 * 3], BF16)
    sGWT = sbuf("sGWT", [128, 8 * 8], BF16)
    sNM1T = sbuf("sNM1T", [DS, 2 * DS], BF16)
    sNM2T = sbuf("sNM2T", [2 * DS, DS], BF16)
    sIR2T = sbuf("sIR2T", [DS, 1], BF16)
    sWoT = sbuf("sWoT", [DS, D], BF16)
    sNW = sbuf("sNW", [128, DS])
    sLB = sbuf("sLB", [128, 8])
    sID = sbuf("sID", [128, 128], BF16)
    sIDF = sbuf("sIDF", [128, 128])
    sEX = sbuf("sEX", [128, 128]); sEXT = sbuf("sEXT", [128, 128])
    sMSL = sbuf("sMSL", [128, 128]); sMSU = sbuf("sMSU", [128, 128])
    sIOR = sbuf("sIOR", [1, 128]); sIOC = sbuf("sIOC", [128, 1])
    sIOCR = sbuf("sIOCR", [128, 1]); sIORB = sbuf("sIORB", [128, 128])
    sONE1 = sbuf("sONE1", [1, 128], BF16)
    sONE128 = sbuf("sONE128", [128, 1], BF16)
    sFA = sbuf("sFA", [128, 1]); sSA = sbuf("sSA", [128, 1])
    sLNFA = sbuf("sLNFA", [128, 1]); sLNSA = sbuf("sLNSA", [128, 1])
    sBF = sbuf("sBF", [128, 1]); sBS = sbuf("sBS", [128, 1])
    sNBF = sbuf("sNBF", [128, 1]); sNBS = sbuf("sNBS", [128, 1])
    sFAC = sbuf("sFAC", [128, 1]); sSAC = sbuf("sSAC", [128, 1])
    sDQFC = sbuf("sDQFC", [128, 1]); sDQSC = sbuf("sDQSC", [128, 1])
    sDKFC = sbuf("sDKFC", [128, 1]); sDKSC = sbuf("sDKSC", [128, 1])
    sNMB1 = sbuf("sNMB1", [2 * DS, 1]); sNMB2 = sbuf("sNMB2", [DS, 1])
    sIRB1 = sbuf("sIRB1", [DS, 1]); sIRB2 = sbuf("sIRB2", [128, 1])
    sDQF = sbuf("sDQF", [128, 128]); sDQS = sbuf("sDQS", [128, 128])
    sEt = sbuf("sEt", [128, 128])
    sNegE_f = sbuf("sNegE_f", [128, 128], BF16)
    sNegET_f = sbuf("sNegET_f", [128, 128], BF16)
    sPosET_f = sbuf("sPosET_f", [128, 128])
    sNegE_s = sbuf("sNegE_s", [128, 128], BF16)
    sNegET_s = sbuf("sNegET_s", [128, 128], BF16)
    sPosET_s = sbuf("sPosET_s", [128, 128])
    sHH = sbuf("sHH", [DS, 2 * DS], BF16)
    sNS0 = sbuf("sNS0", [DS, 1])
    sRelay = sbuf("sRelay", [1, 4], I32)
    sZFlag = sbuf("sZFlag", [1, 2], I32)
    sNonceS = sbuf("sNonceS", [1, 1], I32)
    sFlagRx = sbuf("sFlagRx", [1, 1], I32)
    sFEu = sbuf("sFEu", [1, 1], mybir.dt.uint8)
    sFEb = sbuf("sFEb", [1, 1], BF16)
    sFEc = sbuf("sFEc", [128, 1])
    sIsOdd = sbuf("sIsOdd", [128, 1])
    sMask8 = sbuf("sMask8", [DS, 2 * DS], U8)
    sZeroHH = sbuf("sZeroHH", [DS, 2 * DS])
    sTxHH = sbuf("sTxHH", [DS, 2 * DS])
    sTxNS = sbuf("sTxNS", [DS, 1])
    sRxHH = sbuf("sRxHH", [DS, 2 * DS])
    sRxHHs = sbuf("sRxHHs", [DS, 2 * DS])
    sRxNS = sbuf("sRxNS", [DS, 1])
    sRxNSs = sbuf("sRxNSs", [DS, 1])
    sPart = sbuf("sPart", [1, 16])
    sOHSG = sbuf("sOHSG", [128, 32], BF16)
    sSelT = sbuf("sSelT", [32, 128], BF16)
    sIOTA8 = sbuf("sIOTA8", [128, 8])
    sIOTA8B = sbuf("sIOTA8B", [128, 8])
    sNWrowF = sbuf("sNWrowF", [1, DS])
    sEPS5 = sbuf("sEPS5", [1, 1])
    sNWrowB = sbuf("sNWrowB", [1, DS], BF16)

    aMhT = sbuf("aMhT", [DS, NCH * 2 * C], BF16)
    aD0 = sbuf("aD0", [128, NCH * 128])
    aGqTf = sbuf("aGqTf", [128, NCH * C], BF16)
    aGqTs = sbuf("aGqTs", [128, NCH * C], BF16)
    aQhTf = sbuf("aQhTf", [DS, NCH * C], BF16)
    aQhTs = sbuf("aQhTs", [DS, NCH * C], BF16)
    aKpf = sbuf("aKpf", [128, NCH * DS], BF16)
    aKps = sbuf("aKps", [128, NCH * DS], BF16)
    aVtok = sbuf("aVtok", [128, NCH * DS])
    aNuT = sbuf("aNuT", [DS, NCH * C])
    aPwt = sbuf("aPwt", [128, NCH * 4])
    aQkf = sbuf("aQkf", [128, NCH])
    aQks = sbuf("aQks", [128, NCH])
    aHH = sbuf("aHH", [DS, NCH * 2 * DS], BF16)
    aDb = sbuf("aDb", [128, NCH * 128], BF16)
    aRbf = sbuf("aRbf", [128, NCH * DS], BF16)
    aRbs = sbuf("aRbs", [128, NCH * DS], BF16)
    aUnb = sbuf("aUnb", [128, NCH], BF16)
    aNsTok = sbuf("aNsTok", [128, NCH * DS])

    R = 2
    rg = {}
    def ring(nm, shape, dtype=F32):
        rg[nm] = [sbuf(f"r{r}_{nm}", shape, dtype) for r in range(R)]
    ring("xch", [128, 8 * C])
    ring("xtl", [128, 8 * C])
    ring("xhat", [128, 8 * C], BF16)
    ring("lnt", [128, 8 * C])
    ring("xsq", [128, 8 * C], BF16)
    ring("xbf", [128, 8 * C], BF16)
    ring("mrow", [1, C]); ring("rrow", [1, C])
    ring("mrowb", [1, C], BF16); ring("rrowb", [1, C], BF16)
    ring("kT", [DS, C], BF16); ring("qT", [DS, C], BF16)
    ring("ktT", [DS, C], BF16); ring("vT", [DS, C], BF16)
    ring("ktok", [C, DS]); ring("qtok", [C, DS])
    ring("kttok", [C, DS])
    ring("kb", [C, DS], BF16); ring("qb", [C, DS], BF16)
    ring("ktb", [C, DS], BF16); ring("vb", [C, DS], BF16)
    ring("mb", [C, DS], BF16)
    ring("Rf", [C, 128], BF16); ring("Rs", [C, 128], BF16)
    ring("Ntf", [128, 128], BF16); ring("Nnf", [128, 128], BF16)
    ring("Nts", [128, 128], BF16); ring("Nns", [128, 128], BF16)
    ring("Sp1", [128, 256], BF16); ring("Sp2", [128, 256], BF16)
    ring("Xa", [C, 128], BF16); ring("Xb", [C, 128], BF16)
    ring("nu1b", [128, C], BF16)
    ring("z1b", [DS, C], BF16)
    ring("ucol", [C, 1])
    ring("sc1", [C, DS]); ring("sc2", [C, DS])
    ring("errc", [C, 1])
    ring("urow", [1, C], BF16)
    ring("cbc", [DS, C]); ring("wns", [DS, C]); ring("nsT", [DS, C])
    ring("rf", [C, DS]); ring("rs", [C, DS])
    ring("o1", [C, DS]); ring("o2", [C, DS]); ring("msc", [C, 1])
    ring("ohat", [C, DS], BF16); ring("ohT", [DS, C], BF16)
    ring("scr", [128, 8]); ring("sc8a", [128, 8]); ring("sc8b", [128, 8])
    ring("c1", [128, 1]); ring("c2", [128, 1]); ring("c3", [128, 1])
    ring("ohw", [128, 8], BF16); ring("ohs", [128, 8])

    psTok = psum("psTok", [128, 512])
    psFeat = psum("psFeat", [128, 384])
    psSq = psum("psSq", [128, 256])
    psX = psum("psX", [C, 128])
    psRd = psum("psRd", [C, 128])
    psCh = psum("psCh", [128, 448])
    psTr = psum("psTr", [128, 512], BF16)
    psOut = psum("psOut", [128, 512])

    spin_sem = ctx.enter_context(nc.semaphore("SPIN"))
    spin_sem2 = ctx.enter_context(nc.semaphore("SPIN2"))

    s = Sched(nc)
    _cut = int(os.environ.get("MK_CUT", "0") or "0")
    _pcut = int(os.environ.get("MK_PCUT", "0") or "0")
    _seccur = [1]
    _pstg = [0]

    _ocut = int(os.environ.get("MK_OPCUT", "0") or "0")
    _ocnt = [0]
    _skip = set()
    for _rng in (os.environ.get("MK_SKIP", "") or "").split(","):
        if not _rng:
            continue
        if "-" in _rng:
            _a, _b = _rng.split("-")
            _skip.update(range(int(_a), int(_b) + 1))
        else:
            _skip.add(int(_rng))
    _prtab = os.environ.get("MK_PRTAB", "")

    def A(*a, **kw):
        if _cut and _seccur[0] > _cut:
            return None
        if _pcut and _pstg[0] > _pcut:
            return None
        _ocnt[0] += 1
        if _prtab:
            print("OP", _ocnt[0], a[0],
                  "r:", ",".join(kw.get("reads", ()))[:60],
                  "w:", ",".join(kw.get("writes", ()))[:40],
                  "dma" if kw.get("is_dma") else "")
        if _ocut and _ocnt[0] > _ocut:
            return None
        if _ocnt[0] in _skip:
            return None
        return s.add(*a, **kw)
    out_dmas = []

    def N(ap_):
        return ap_.tensor.name

    for dst, src in [(sWkT, dWkT), (sWvT, dWvT), (sWqT, dWqT),
                     (sIR1T, dIR1T), (sPWT, dPWT), (sGWT, dGWT)]:
        A("POOL", (lambda d_, s_: lambda e: e.dma_start(
            out=d_, in_=s_.rearrange("(g p) i -> p g i", p=128)))(dst, src),
          writes=[N(dst)], is_dma=True)
    for dst, src in [(sNM1T, dNM1T), (sNM2T, dNM2T), (sIR2T, dIR2T),
                     (sWoT, dWoT)]:
        A("POOL", (lambda d_, s_: lambda e: e.dma_start(out=d_, in_=s_))(dst, src),
          writes=[N(dst)], is_dma=True)
    for dst, src in [(sID, dID), (sEX, dEX), (sEXT, dEXT), (sMSL, dMSL),
                     (sMSU, dMSU), (sIOR, dIOR), (sIOC, dIOC), (sIOCR, dIOCR),
                     (sNMB1, dNMB1), (sNMB2, dNMB2), (sIRB1, dIRB1),
                     (sRelay, dRELAY), (sNWrowF, dNW)]:
        A("SP", (lambda d_, s_: lambda e: e.dma_start(out=d_, in_=s_))(dst, src),
          writes=[N(dst)], is_dma=True)
    A("SP", lambda e: e.dma_start(out=sFA, in_=_bpar(dFA)), writes=["sFA"], is_dma=True)
    A("SP", lambda e: e.dma_start(out=sSA, in_=_bpar(dSA)), writes=["sSA"], is_dma=True)
    A("SP", lambda e: e.dma_start(out=sIRB2, in_=_bpar(dIRB2)), writes=["sIRB2"], is_dma=True)
    A("SP", lambda e: e.dma_start(out=sLB, in_=_bpar(dLB)), writes=["sLB"], is_dma=True)
    A("SP", lambda e: e.dma_start(out=sIsOdd, in_=_bpar(dISODD)), writes=["sIsOdd"], is_dma=True)
    A("SP", lambda e: e.dma_start(out=sNonceS, in_=dRELAY[0:1, 3:4]), writes=["sNonceS"], is_dma=True)
    A("SP", lambda e: e.dma_start(out=sIORB, in_=_bpar(dIOR)), writes=["sIORB"], is_dma=True)
    A("SP", lambda e: e.dma_start(out=sIOTA8, in_=dataclasses.replace(
        dIOR, ap=[[0, 128], [1, 8]])), writes=["sIOTA8"], is_dma=True)

    A("DVE", lambda e: e.memset(sONE1, 1.0), writes=["sONE1"])
    A("DVE", lambda e: e.memset(sONE128, 1.0), writes=["sONE128"])
    A("DVE", lambda e: e.memset(sPart, 0.0), writes=["sPart"])
    A("DVE", lambda e: e.memset(sEPS5, 1e-5), writes=["sEPS5"])
    A("DVE", lambda e: e.memset(sZeroHH, 0.0), writes=["sZeroHH"])
    A("DVE", lambda e: e.memset(sZFlag, 0), writes=["sZFlag"])
    A("DVE", lambda e: e.memset(sFlagRx, 0), writes=["sFlagRx"])
    def emit_zslot1(e):
        slot = e.value_load(sRelay[0:1, 0:1])
        return e.dma_start(out=shState[bass.ts(slot, 1), 0:128 * DS],
                           in_=sZeroHH)
    A("SP", emit_zslot1, reads=["sZeroHH", "sRelay"], writes=["zs1"], is_dma=True)
    def emit_zslot2(e):
        slot = e.value_load(sRelay[0:1, 0:1])
        return e.dma_start(out=shState[bass.ts(slot, 1), 128 * DS:130 * DS],
                           in_=sZeroHH[0:DS, 0:2])
    A("SP", emit_zslot2, reads=["sZeroHH", "sRelay"], writes=["zs2"], is_dma=True)
    def emit_zflag(e):
        slot = e.value_load(sRelay[0:1, 0:1])
        return e.dma_start(out=shFlag[bass.ts(slot, 1), :], in_=sZFlag)
    A("SP", emit_zflag, reads=["sZFlag", "sRelay"], writes=["zf"], is_dma=True)
    A("DVE", lambda e: e.tensor_copy(out=sMask8,
                                     in_=_repf(sIsOdd[0:DS, :], 2 * DS)),
      reads=["sIsOdd"], writes=["sMask8"])
    A("DVE", lambda e: e.tensor_scalar(sIOTA8B, sIOTA8, BIG, None, ALU.add),
      reads=["sIOTA8"], writes=["sIOTA8B"])
    A("DVE", lambda e: e.tensor_copy(out=sNWrowB, in_=sNWrowF),
      reads=["sNWrowF"], writes=["sNWrowB"])
    A("PE", lambda e: e.matmul(psFeat[:, 0:DS], sONE1, sNWrowB, start=True, stop=True),
      reads=["sONE1", "sNWrowB"], writes=["psFeat"])
    A("DVE", lambda e: e.tensor_copy(out=sNW, in_=psFeat[:, 0:DS]),
      reads=["psFeat"], writes=["sNW"])
    A("DVE", lambda e: e.tensor_copy(out=sIDF, in_=sID),
      reads=["sID"], writes=["sIDF"])

    A("ACT", lambda e: e.activation(sLNFA, sFA, AF.Ln), reads=["sFA"], writes=["sLNFA"])
    A("ACT", lambda e: e.activation(sLNSA, sSA, AF.Ln), reads=["sSA"], writes=["sLNSA"])
    A("DVE", lambda e: e.tensor_scalar(sBF, sFA, -1.0, 1.0, ALU.mult, ALU.add),
      reads=["sFA"], writes=["sBF"])
    A("DVE", lambda e: e.tensor_scalar(sBS, sSA, -1.0, 1.0, ALU.mult, ALU.add),
      reads=["sSA"], writes=["sBS"])
    A("DVE", lambda e: e.tensor_scalar(sNBF, sBF, -1.0, None, ALU.mult),
      reads=["sBF"], writes=["sNBF"])
    A("DVE", lambda e: e.tensor_scalar(sNBS, sBS, -1.0, None, ALU.mult),
      reads=["sBS"], writes=["sNBS"])
    A("ACT", lambda e: e.activation(sFAC, sLNFA, AF.Exp, scale=float(C)),
      reads=["sLNFA"], writes=["sFAC"])
    A("ACT", lambda e: e.activation(sSAC, sLNSA, AF.Exp, scale=float(C)),
      reads=["sLNSA"], writes=["sSAC"])
    A("ACT", lambda e: e.activation(sDQFC, sIOC, AF.Exp, scale=sLNFA),
      reads=["sIOC", "sLNFA"], writes=["sDQFC"])
    A("ACT", lambda e: e.activation(sDQSC, sIOC, AF.Exp, scale=sLNSA),
      reads=["sIOC", "sLNSA"], writes=["sDQSC"])
    A("ACT", lambda e: e.activation(sDKFC, sIOCR, AF.Exp, scale=sLNFA),
      reads=["sIOCR", "sLNFA"], writes=["sDKFC"])
    A("ACT", lambda e: e.activation(sDKSC, sIOCR, AF.Exp, scale=sLNSA),
      reads=["sIOCR", "sLNSA"], writes=["sDKSC"])
    A("DVE", lambda e: e.tensor_scalar(sDKFC, sDKFC, sBF, None, ALU.mult),
      reads=["sBF"], writes=["sDKFC"])
    A("DVE", lambda e: e.tensor_scalar(sDKSC, sDKSC, sBS, None, ALU.mult),
      reads=["sBS"], writes=["sDKSC"])
    A("ACT", lambda e: e.activation(sDQF, sIORB, AF.Exp, scale=sLNFA),
      reads=["sIORB", "sLNFA"], writes=["sDQF"])
    A("ACT", lambda e: e.activation(sDQS, sIORB, AF.Exp, scale=sLNSA),
      reads=["sIORB", "sLNSA"], writes=["sDQS"])
    for tag, lncol, bcol, nbcol, negE, negET, posET in [
        ("f", sLNFA, sBF, sNBF, sNegE_f, sNegET_f, sPosET_f),
        ("s", sLNSA, sBS, sNBS, sNegE_s, sNegET_s, sPosET_s),
    ]:
        A("ACT", (lambda ln_: lambda e: e.activation(sEt, sEX, AF.Exp, scale=ln_))(lncol),
          reads=["sEX", N(lncol)], writes=["sEt"])
        A("DVE", (lambda nb_, d_: lambda e: e.scalar_tensor_tensor(
            d_, sEt, nb_, sMSL, ALU.mult, ALU.mult))(nbcol, negE),
          reads=["sEt", N(nbcol), "sMSL"], writes=[N(negE)])
        A("ACT", (lambda ln_: lambda e: e.activation(sEt, sEXT, AF.Exp, scale=ln_))(lncol),
          reads=["sEXT", N(lncol)], writes=["sEt"])
        A("DVE", (lambda nb_, d_: lambda e: e.scalar_tensor_tensor(
            d_, sEt, nb_, sMSU, ALU.mult, ALU.mult))(nbcol, negET),
          reads=["sEt", N(nbcol), "sMSU"], writes=[N(negET)])
        A("DVE", (lambda b_, d_: lambda e: e.scalar_tensor_tensor(
            d_, sEt, b_, sMSU, ALU.mult, ALU.mult))(bcol, posET),
          reads=["sEt", N(bcol), "sMSU"], writes=[N(posET)])

    def emit_ln(xsrc, xkey, r, i, save_stats):
        xv = xsrc.rearrange("p (g t) -> p g t", g=8)
        xbf, xsq, lnt = rg["xbf"][r], rg["xsq"][r], rg["lnt"][r]
        mrow, rrow = rg["mrow"][r], rg["rrow"][r]
        mrowb, rrowb = rg["mrowb"][r], rg["rrowb"][r]
        xhat = rg["xhat"][r]
        A("DVE", lambda e: e.tensor_copy(
            out=xbf.rearrange("p (g t) -> p g t", g=8), in_=xv),
          reads=[xkey], writes=[N(xbf)])
        A("ACT", lambda e: e.activation(
            xsq.rearrange("p (g t) -> p g t", g=8), xv, AF.Square),
          reads=[xkey], writes=[N(xsq)])
        def mm_sums(e):
            last = None
            for g in range(8):
                last = e.matmul(psFeat[0:1, 128:128 + C], sONE128,
                                xbf[:, g * C:(g + 1) * C],
                                start=(g == 0), stop=(g == 7))
            for g in range(8):
                last = e.matmul(psFeat[0:1, 256:256 + C], sONE128,
                                xsq[:, g * C:(g + 1) * C],
                                start=(g == 0), stop=(g == 7))
            return last
        A("PE", mm_sums, reads=[N(xbf), N(xsq), "sONE128"], writes=["psFeat"])
        A("DVE", lambda e: e.tensor_scalar(mrow, psFeat[0:1, 128:128 + C],
                                           1.0 / D, None, ALU.mult),
          reads=["psFeat"], writes=[N(mrow)])
        A("DVE", lambda e: e.scalar_tensor_tensor(
            rrow, mrow, -1.0, mrow, ALU.mult, ALU.mult),
          reads=[N(mrow)], writes=[N(rrow)])
        A("DVE", lambda e: e.scalar_tensor_tensor(
            rrow, psFeat[0:1, 256:256 + C], 1.0 / D, rrow, ALU.mult, ALU.add),
          reads=["psFeat", N(rrow)], writes=[N(rrow)])
        A("ACT", lambda e: e.activation(rrow, rrow, AF.Ln, bias=sEPS5),
          reads=[N(rrow), "sEPS5"], writes=[N(rrow)])
        A("ACT", lambda e: e.activation(rrow, rrow, AF.Exp, scale=-0.5),
          reads=[N(rrow)], writes=[N(rrow)])
        if save_stats:
            cs = i * C
            op = A("SP", (lambda cs_, m_: lambda e: e.dma_start(
                out=oSTATS[0:1, cs_:cs_ + C], in_=m_))(cs, mrow),
                reads=[N(mrow)], is_dma=True)
            out_dmas.append(op)
            op = A("SP", (lambda cs_, m_: lambda e: e.dma_start(
                out=oSTATS[1:2, cs_:cs_ + C], in_=m_))(cs, rrow),
                reads=[N(rrow)], is_dma=True)
            out_dmas.append(op)
        A("DVE", lambda e: e.tensor_copy(out=mrowb, in_=mrow),
          reads=[N(mrow)], writes=[N(mrowb)])
        A("DVE", lambda e: e.tensor_copy(out=rrowb, in_=rrow),
          reads=[N(rrow)], writes=[N(rrowb)])
        def mm_bc(e):
            e.matmul(psFeat[:, 128:128 + C], sONE1, mrowb, start=True, stop=True)
            return e.matmul(psFeat[:, 256:256 + C], sONE1, rrowb,
                            start=True, stop=True)
        A("PE", mm_bc, reads=[N(mrowb), N(rrowb), "sONE1"], writes=["psFeat"])
        A("DVE", lambda e: e.tensor_tensor(
            lnt.rearrange("p (g t) -> p g t", g=8), xv,
            _rep(psFeat[:, 128:128 + C], 8), ALU.subtract),
          reads=[xkey, "psFeat"], writes=[N(lnt)])
        A("DVE", lambda e: e.tensor_tensor(
            xhat.rearrange("p (g t) -> p g t", g=8),
            lnt.rearrange("p (g t) -> p g t", g=8),
            _rep(psFeat[:, 256:256 + C], 8), ALU.mult),
          reads=[N(lnt), "psFeat"], writes=[N(xhat)])
        return xhat

    def emit_prep(i):
        _pstg[0] = 10
        r = i % R
        cs = i * C
        xch = rg["xch"][r]
        A("SP", (lambda cs_, x_: lambda e: e.dma_start(
            out=x_.rearrange("p (g t) -> p g t", g=8),
            in_=dX_v[:, :, cs_:cs_ + C]))(cs, xch),
          writes=[N(xch)], is_dma=True)
        xhat = emit_ln(xch, N(xch), r, i, save_stats=False)
        _pstg[0] = 20
        def mm_proj(e):
            last = None
            for dst, w_, wd_ in [(psTok[:, 0:DS], sWkT, DS),
                                 (psTok[:, DS:2 * DS], sWvT, DS),
                                 (psTok[:, 2 * DS:3 * DS], sWqT, DS),
                                 (psTok[:, 192:195], sPWT, 3)]:
                for g in range(8):
                    last = e.matmul(dst, xhat[:, g * C:(g + 1) * C],
                                    w_[:, g * wd_:(g + 1) * wd_],
                                    start=(g == 0), stop=(g == 7))
            return last
        A("PE", mm_proj, reads=[N(xhat), "sWkT", "sWvT", "sWqT", "sPWT"],
          writes=["psTok"])
        def mm_imp(e):
            last = None
            for g in range(8):
                last = e.matmul(psFeat[0:DS, 0:C],
                                sIR1T[:, g * DS:(g + 1) * DS],
                                xhat[:, g * C:(g + 1) * C],
                                start=(g == 0), stop=(g == 7))
            return last
        A("PE", mm_imp, reads=[N(xhat), "sIR1T"], writes=["psFeat"])
        _pstg[0] = 30
        ktok, qtok = rg["ktok"][r], rg["qtok"][r]
        vtok = aVtok[:, i * DS:(i + 1) * DS]
        for nm, off, dst in [("k", 0, ktok), ("q", 2 * DS, qtok)]:
            _pstg[0] = 31
            c1, c2 = rg["c1"][r], rg["c2"][r]
            A("ACT", (lambda o_, c_: lambda e: e.activation(
                rg["sc1"][r], psTok[:, o_:o_ + DS], AF.Square, accum_out=c_))(off, c1),
              reads=["psTok"], writes=[N(rg["sc1"][r]), N(c1)])
            A("ACT", (lambda c_: lambda e: e.activation(c_, c_, AF.Ln))(c1),
              reads=[], writes=[N(c1)])
            A("ACT", (lambda c_: lambda e: e.activation(c_, c_, AF.Exp, scale=0.5))(c1),
              reads=[], writes=[N(c1)])
            _pstg[0] = 32
            A("DVE", (lambda c_: lambda e: e.tensor_scalar(
                c_, c_, 1e-12, None, ALU.max))(c1),
              reads=[], writes=[N(c1)])
            A("DVE", (lambda a_, b_: lambda e: e.reciprocal(b_, a_))(c1, c2),
              reads=[N(c1)], writes=[N(c2)])
            _v95 = os.environ.get("MK_V95", "")
            if _v95 == "imm":
                A("DVE", (lambda o_, d_: lambda e: e.tensor_scalar(
                    d_, psTok[:, o_:o_ + DS], 1.0, None, ALU.mult))(off, dst),
                  reads=["psTok"], writes=[N(dst)])
            elif _v95 == "sb":
                A("DVE", (lambda d_, c_: lambda e: e.tensor_scalar(
                    d_, rg["sc1"][r], c_, None, ALU.mult))(dst, c2),
                  reads=[N(rg["sc1"][r]), N(c2)], writes=[N(dst)])
            elif _v95 == "act":
                A("ACT", (lambda o_, d_, c_: lambda e: e.activation(
                    d_, psTok[:, o_:o_ + DS], AF.Copy, scale=c_))(off, dst, c2),
                  reads=["psTok", N(c2)], writes=[N(dst)])
            elif _v95 == "copy":
                A("DVE", (lambda o_, d_: lambda e: e.tensor_copy(
                    out=d_, in_=psTok[:, o_:o_ + DS]))(off, dst),
                  reads=["psTok"], writes=[N(dst)])
            else:
                A("DVE", (lambda o_, d_, c_: lambda e: e.tensor_scalar(
                    d_, psTok[:, o_:o_ + DS], c_, None, ALU.mult))(off, dst, c2),
                  reads=["psTok", N(c2)], writes=[N(dst)])
        _pstg[0] = 34
        A("ACT", (lambda v_: lambda e: e.activation(
            v_, psTok[:, DS:2 * DS], AF.Copy))(vtok),
          reads=["psTok"], writes=[f"aVtok@{i}"])
        _pstg[0] = 36
        pwt = aPwt[:, i * 4:i * 4 + 4]
        c3 = rg["c3"][r]
        A("DVE", lambda e: e.reduce_max(c3, psTok[:, 192:195], AX.X),
          reads=["psTok"], writes=[N(c3)])
        A("DVE", lambda e: e.tensor_scalar(rg["c1"][r], c3, -1.0, None, ALU.mult),
          reads=[N(c3)], writes=[N(rg["c1"][r])])
        A("ACT", (lambda p_: lambda e: e.activation(
            p_[:, 0:3], psTok[:, 192:195], AF.Exp, bias=rg["c1"][r]))(pwt),
          reads=["psTok", N(rg["c1"][r])], writes=[f"aPwt@{i}"])
        A("DVE", (lambda p_: lambda e: e.reduce_sum(c3, p_[:, 0:3], AX.X))(pwt),
          reads=[f"aPwt@{i}"], writes=[N(c3)])
        A("DVE", lambda e: e.reciprocal(c3, c3), reads=[], writes=[N(c3)])
        A("DVE", (lambda p_: lambda e: e.tensor_scalar(
            p_[:, 0:3], p_[:, 0:3], c3, None, ALU.mult))(pwt),
          reads=[N(c3)], writes=[f"aPwt@{i}"])
        _pstg[0] = 40
        ucol = rg["ucol"][r]
        A("ACT", lambda e: e.activation(rg["z1b"][r], psFeat[0:DS, 0:C], AF.Gelu,
                                        bias=sIRB1),
          reads=["psFeat", "sIRB1"], writes=[N(rg["z1b"][r])])
        A("PE", lambda e: e.matmul(psRd[:, 0:1], rg["z1b"][r], sIR2T,
                                   start=True, stop=True),
          reads=[N(rg["z1b"][r]), "sIR2T"], writes=["psRd"])
        A("DVE", lambda e: e.tensor_scalar(ucol, psRd[:, 0:1], sIRB2, 0.0,
                                           ALU.add, ALU.is_gt),
          reads=["psRd", "sIRB2"], writes=[N(ucol)])
        kttok = rg["kttok"][r]
        A("DVE", lambda e: e.tensor_scalar(kttok, ktok, ucol, None, ALU.mult),
          reads=[N(ktok), N(ucol)], writes=[N(kttok)])
        A("DVE", (lambda kp_: lambda e: e.tensor_scalar(
            kp_, ktok, sDKFC, None, ALU.mult))(aKpf[:, i * DS:(i + 1) * DS]),
          reads=[N(ktok), "sDKFC"], writes=[f"aKpf@{i}"])
        A("DVE", (lambda kp_: lambda e: e.tensor_scalar(
            kp_, kttok, sDKSC, None, ALU.mult))(aKps[:, i * DS:(i + 1) * DS]),
          reads=[N(kttok), "sDKSC"], writes=[f"aKps@{i}"])
        A("DVE", lambda e: e.tensor_tensor(rg["sc1"][r], qtok, ktok, ALU.mult),
          reads=[N(qtok), N(ktok)], writes=[N(rg["sc1"][r])])
        A("DVE", (lambda q_: lambda e: e.reduce_sum(q_, rg["sc1"][r], AX.X))(
            aQkf[:, i:i + 1]),
          reads=[N(rg["sc1"][r])], writes=[f"aQkf@{i}"])
        A("DVE", lambda e: e.tensor_tensor(rg["sc2"][r], qtok, kttok, ALU.mult),
          reads=[N(qtok), N(kttok)], writes=[N(rg["sc2"][r])])
        A("DVE", (lambda q_: lambda e: e.reduce_sum(q_, rg["sc2"][r], AX.X))(
            aQks[:, i:i + 1]),
          reads=[N(rg["sc2"][r])], writes=[f"aQks@{i}"])
        _pstg[0] = 50
        Rf, Rs = rg["Rf"][r], rg["Rs"][r]
        A("DVE", (lambda v_: lambda e: e.tensor_copy(out=Rf[:, 0:DS], in_=v_))(vtok),
          reads=[f"aVtok@{i}"], writes=[N(Rf)])
        A("DVE", lambda e: e.tensor_scalar(Rf[:, DS:128], ktok, sDQFC, None, ALU.mult),
          reads=[N(ktok), "sDQFC"], writes=[N(Rf)])
        A("DVE", (lambda v_: lambda e: e.tensor_copy(out=Rs[:, 0:DS], in_=v_))(vtok),
          reads=[f"aVtok@{i}"], writes=[N(Rs)])
        A("DVE", lambda e: e.tensor_scalar(Rs[:, DS:128], kttok, sDQSC, None, ALU.mult),
          reads=[N(kttok), "sDQSC"], writes=[N(Rs)])
        for nm, srcap, srck in [("kb", ktok, N(ktok)), ("qb", qtok, N(qtok)),
                                ("ktb", kttok, N(kttok)),
                                ("vb", vtok, f"aVtok@{i}")]:
            A("DVE", (lambda d_, s_: lambda e: e.tensor_copy(out=d_, in_=s_))(
                rg[nm][r], srcap),
              reads=[srck], writes=[N(rg[nm][r])])
        def mm_tr1(e):
            e.transpose(psTr[0:DS, 0:C], rg["kb"][r], sID)
            return e.transpose(psTr[0:DS, C:2 * C], rg["qb"][r], sID)
        A("PE", mm_tr1, reads=[N(rg["kb"][r]), N(rg["qb"][r]), "sID"],
          writes=["psTr"])
        A("DVE", lambda e: e.tensor_copy(out=rg["kT"][r], in_=psTr[0:DS, 0:C]),
          reads=["psTr"], writes=[N(rg["kT"][r])])
        A("DVE", lambda e: e.tensor_copy(out=rg["qT"][r], in_=psTr[0:DS, C:2 * C]),
          reads=["psTr"], writes=[N(rg["qT"][r])])
        def mm_tr2(e):
            e.transpose(psTr[0:DS, 2 * C:3 * C], rg["ktb"][r], sID)
            return e.transpose(psTr[0:DS, 3 * C:4 * C], rg["vb"][r], sID)
        A("PE", mm_tr2, reads=[N(rg["ktb"][r]), N(rg["vb"][r]), "sID"],
          writes=["psTr"])
        A("DVE", lambda e: e.tensor_copy(out=rg["ktT"][r], in_=psTr[0:DS, 2 * C:3 * C]),
          reads=["psTr"], writes=[N(rg["ktT"][r])])
        A("DVE", lambda e: e.tensor_copy(out=rg["vT"][r], in_=psTr[0:DS, 3 * C:4 * C]),
          reads=["psTr"], writes=[N(rg["vT"][r])])
        A("DVE", (lambda q_: lambda e: e.tensor_tensor(
            q_, rg["qT"][r], sDQF[0:DS, :], ALU.mult))(aQhTf[:, cs:cs + C]),
          reads=[N(rg["qT"][r]), "sDQF"], writes=[f"aQhTf@{i}"])
        A("DVE", (lambda q_: lambda e: e.tensor_tensor(
            q_, rg["qT"][r], sDQS[0:DS, :], ALU.mult))(aQhTs[:, cs:cs + C]),
          reads=[N(rg["qT"][r]), "sDQS"], writes=[f"aQhTs@{i}"])
        _pstg[0] = 60
        def mm_A(e):
            e.matmul(psSq[:, 0:128], rg["kT"][r], rg["kT"][r], start=True, stop=True)
            return e.matmul(psSq[:, 128:256], rg["kT"][r], rg["qT"][r],
                            start=True, stop=True)
        A("PE", mm_A, reads=[N(rg["kT"][r]), N(rg["qT"][r])], writes=["psSq"])
        A("PE", lambda e: e.matmul(psX[:, 0:128], rg["ktT"][r], rg["ktT"][r],
                                   start=True, stop=True),
          reads=[N(rg["ktT"][r])], writes=["psX"])
        A("DVE", lambda e: e.tensor_tensor(rg["Ntf"][r], psSq[:, 0:128],
                                           sNegET_f, ALU.mult),
          reads=["psSq", "sNegET_f"], writes=[N(rg["Ntf"][r])])
        A("DVE", lambda e: e.tensor_tensor(rg["Nnf"][r], psSq[:, 0:128],
                                           sNegE_f, ALU.mult),
          reads=["psSq", "sNegE_f"], writes=[N(rg["Nnf"][r])])
        A("DVE", (lambda g_: lambda e: e.tensor_tensor(
            g_, psSq[:, 128:256], sPosET_f, ALU.mult))(aGqTf[:, cs:cs + C]),
          reads=["psSq", "sPosET_f"], writes=[f"aGqTf@{i}"])
        A("DVE", (lambda g_: lambda e: e.scalar_tensor_tensor(
            g_, psSq[:, 128:256], rg["ucol"][r], sPosET_s,
            ALU.mult, ALU.mult))(aGqTs[:, cs:cs + C]),
          reads=["psSq", N(rg["ucol"][r]), "sPosET_s"], writes=[f"aGqTs@{i}"])
        A("DVE", lambda e: e.tensor_tensor(rg["Nts"][r], psX[:, 0:128],
                                           sNegET_s, ALU.mult),
          reads=["psX", "sNegET_s"], writes=[N(rg["Nts"][r])])
        A("DVE", lambda e: e.tensor_tensor(rg["Nns"][r], psX[:, 0:128],
                                           sNegE_s, ALU.mult),
          reads=["psX", "sNegE_s"], writes=[N(rg["Nns"][r])])
        _pstg[0] = 70
        A("PE", lambda e: e.matmul(psFeat[:, 0:C], sNM1T, rg["vT"][r],
                                   start=True, stop=True),
          reads=[N(rg["vT"][r]), "sNM1T"], writes=["psFeat"])
        A("ACT", lambda e: e.activation(rg["nu1b"][r], psFeat[:, 0:C], AF.Gelu,
                                        bias=sNMB1),
          reads=["psFeat", "sNMB1"], writes=[N(rg["nu1b"][r])])
        A("PE", lambda e: e.matmul(psFeat[0:DS, 128:128 + C], sNM2T,
                                   rg["nu1b"][r], start=True, stop=True),
          reads=[N(rg["nu1b"][r]), "sNM2T"], writes=["psFeat"])
        A("DVE", (lambda n_: lambda e: e.tensor_scalar(
            n_, psFeat[0:DS, 128:128 + C], sNMB2, None, ALU.add))(
            aNuT[:, cs:cs + C]),
          reads=["psFeat", "sNMB2"], writes=[f"aNuT@{i}"])
        _pstg[0] = 80
        for st in ("f", "s"):
            Sp = [rg["Sp1"][r], rg["Sp2"][r]]
            Xp = [rg["Xa"][r], rg["Xb"][r]]
            Rb = rg["Rf" if st == "f" else "Rs"][r]
            cur_t = rg["Ntf" if st == "f" else "Nts"][r]
            cur_n = rg["Nnf" if st == "f" else "Nns"][r]
            ct_key, cn_key = N(cur_t), N(cur_n)
            src, src_key = Rb, N(Rb)
            for lev in range(7):
                A("PE", (lambda t_, s_: lambda e: e.matmul(
                    psX, t_, s_, start=True, stop=True))(cur_t, src),
                  reads=[ct_key, src_key], writes=["psX"])
                dst = Xp[lev % 2]
                A("DVE", (lambda d_, s_: lambda e: e.tensor_tensor(
                    d_, psX, s_, ALU.add))(dst, src),
                  reads=["psX", src_key], writes=[N(dst)])
                src, src_key = dst, N(dst)
                if lev < 6:
                    so = Sp[lev % 2]
                    def mm_sq(e, t_=cur_t, n_=cur_n):
                        e.matmul(psSq[:, 0:128], t_, n_, start=True, stop=True)
                        return e.matmul(psSq[:, 128:256], n_, t_,
                                        start=True, stop=True)
                    A("PE", mm_sq, reads=[ct_key, cn_key], writes=["psSq"])
                    A("DVE", (lambda o_: lambda e: e.tensor_copy(
                        out=o_, in_=psSq))(so),
                      reads=["psSq"], writes=[N(so)])
                    cur_t = so[:, 0:128]
                    cur_n = so[:, 128:256]
                    ct_key = cn_key = N(so)
            off = 0 if st == "f" else DS
            A("DVE", (lambda s_, o_: lambda e: e.tensor_copy(
                out=aD0[:, i * 128 + o_:i * 128 + o_ + DS], in_=s_[:, 0:DS]))(
                src, off),
              reads=[src_key], writes=[f"aD0@{i}"])
            mb = rg["mb"][r]
            A("DVE", (lambda s_: lambda e: e.tensor_copy(
                out=mb, in_=s_[:, DS:128]))(src),
              reads=[src_key], writes=[N(mb)])
            poff = 0 if st == "f" else C
            A("PE", (lambda po_: lambda e: e.transpose(
                psTr[0:DS, po_:po_ + C], rg["mb"][r], sID))(poff),
              reads=[N(mb), "sID"], writes=["psTr"])
            moff = i * 2 * C + (0 if st == "f" else C)
            A("DVE", (lambda mo_, po_: lambda e: e.tensor_scalar(
                aMhT[:, mo_:mo_ + C], psTr[0:DS, po_:po_ + C],
                -1.0, None, ALU.mult))(moff, poff),
              reads=["psTr"], writes=[f"aMhT@{i}"])

    _seccur[0] = 2
    for i in range(int(os.environ.get("MK_NCHA", "0") or "0") or NCH):
        emit_prep(i)
    _pstg[0] = 0
    _seccur[0] = 3

    def mk_spin(flag_col):
        sem = spin_sem if flag_col == 0 else spin_sem2
        def emit_spin(e):
            part = e.value_load(sRelay[0:1, 1:2])
            dowait = e.value_load(sRelay[0:1, 2:3])
            nonce = e.value_load(sRelay[0:1, 3:4])
            with e.register(f"r_cnt{flag_col}") as r_cnt, \
                 e.register(f"r_cond{flag_col}") as r_cond:
                e.reg_mov(r_cnt, 0)
                e.reg_alu(r_cond, dowait, 0, ALU.not_equal)
                with e.While(lambda: r_cond):
                    e.dma_start(out=sFlagRx,
                                in_=shFlag[bass.ts(part, 1),
                                           flag_col:flag_col + 1]
                                ).then_inc(sem, 16)
                    e.reg_add(r_cnt, r_cnt, 16)
                    e.wait_ge(sem, r_cnt)
                    v = e.value_load(sFlagRx)
                    e.reg_alu(r_cond, v, nonce, ALU.not_equal)
                    with e.If_cmp(r_cnt, 16 * 4000000, "IS_GE"):
                        e.reg_mov(r_cond, 0)
            return e.nop()
        return emit_spin
    A("SP", mk_spin(0), reads=["sRelay", "zf"], writes=["spun1", "sFlagRx"], tag="spin1")
    def emit_rx1(e):
        part = e.value_load(sRelay[0:1, 1:2])
        return e.dma_start(out=sRxHH, in_=shState[bass.ts(part, 1), 0:128 * DS])
    A("SP", emit_rx1, reads=["spun1", "sRelay", "zs1"], writes=["sRxHH"], is_dma=True)
    A("DVE", lambda e: e.select(sRxHHs, sMask8, sRxHH, sZeroHH),
      reads=["sRxHH", "sMask8", "sZeroHH"], writes=["sRxHHs"])
    A("DVE", lambda e: e.tensor_tensor(sFEu, sFlagRx, sNonceS, ALU.is_equal),
      reads=["sFlagRx", "sNonceS"], writes=["sFEu"])
    A("DVE", lambda e: e.tensor_copy(out=sFEb, in_=sFEu),
      reads=["sFEu"], writes=["sFEb"])
    A("PE", lambda e: e.matmul(psCh[:, 440:441], sONE1, sFEb,
                               start=True, stop=True),
      reads=["sONE1", "sFEb"], writes=["psCh"])
    A("DVE", lambda e: e.tensor_copy(out=sFEc, in_=psCh[:, 440:441]),
      reads=["psCh"], writes=["sFEc"])
    A("DVE", lambda e: e.tensor_scalar(sHH, sRxHHs, sFEc[0:DS, :], None,
                                       ALU.mult),
      reads=["sRxHHs", "sFEc"], writes=["sHH"])

    for i in range(NCH):
        A("DVE", (lambda i_: lambda e: e.tensor_copy(
            out=aHH[:, i_ * 2 * DS:(i_ + 1) * 2 * DS], in_=sHH))(i),
          reads=["sHH"], writes=[f"aHH@{i}"])
        def mm_d(e, i=i):
            e.matmul(psCh[0:C, 0:DS], aMhT[:, i * 2 * C:i * 2 * C + C],
                     sHH[:, 0:DS], start=True, stop=True)
            return e.matmul(psCh[0:C, DS:2 * DS],
                            aMhT[:, i * 2 * C + C:i * 2 * C + 2 * C],
                            sHH[:, DS:2 * DS], start=True, stop=True)
        A("PE", mm_d, reads=[f"aMhT@{i}", "sHH"], writes=["psCh"])
        A("DVE", (lambda i_: lambda e: e.tensor_tensor(
            aDb[:, i_ * 128:(i_ + 1) * 128], psCh[0:C, 0:2 * DS],
            aD0[:, i_ * 128:(i_ + 1) * 128], ALU.add))(i),
          reads=["psCh", f"aD0@{i}"], writes=[f"aDb@{i}"])
        def mm_H(e, i=i):
            e.matmul(psCh[0:DS, 2 * DS:3 * DS], aKpf[:, i * DS:(i + 1) * DS],
                     aDb[:, i * 128:i * 128 + DS], start=True, stop=True)
            return e.matmul(psCh[0:DS, 3 * DS:4 * DS],
                            aKps[:, i * DS:(i + 1) * DS],
                            aDb[:, i * 128 + DS:(i + 1) * 128],
                            start=True, stop=True)
        A("PE", mm_H, reads=[f"aKpf@{i}", f"aKps@{i}", f"aDb@{i}"],
          writes=["psCh"])
        A("DVE", lambda e: e.scalar_tensor_tensor(
            sHH[:, 0:DS], sHH[:, 0:DS], sFAC[0:DS, :],
            psCh[0:DS, 2 * DS:3 * DS], ALU.mult, ALU.add),
          reads=["psCh", "sFAC"], writes=["sHH"])
        A("DVE", lambda e: e.scalar_tensor_tensor(
            sHH[:, DS:2 * DS], sHH[:, DS:2 * DS], sSAC[0:DS, :],
            psCh[0:DS, 3 * DS:4 * DS], ALU.mult, ALU.add),
          reads=["psCh", "sSAC"], writes=["sHH"])

    A("DVE", lambda e: e.tensor_copy(out=sTxHH, in_=sHH),
      reads=["sHH"], writes=["sTxHH"])
    def emit_tx1(e):
        slot = e.value_load(sRelay[0:1, 0:1])
        return e.dma_start(out=shState[bass.ts(slot, 1), 0:128 * DS], in_=sTxHH)
    A("SP", emit_tx1, reads=["sTxHH", "sRelay"], writes=["txd1"], is_dma=True)
    def emit_txf1(e):
        slot = e.value_load(sRelay[0:1, 0:1])
        return e.dma_start(out=shFlag[bass.ts(slot, 1), 0:1], in_=sNonceS)
    A("SP", emit_txf1, reads=["txd1", "sNonceS", "sRelay"], writes=["txf1"],
      is_dma=True)

    _seccur[0] = 4
    for i in range(NCH):
        def mm_rd(e, i=i):
            e.matmul(psRd[:, 0:DS], aQhTf[:, i * C:(i + 1) * C],
                     aHH[:, i * 2 * DS:i * 2 * DS + DS], start=True, stop=False)
            e.matmul(psRd[:, 0:DS], aGqTf[:, i * C:(i + 1) * C],
                     aDb[:, i * 128:i * 128 + DS], start=False, stop=True)
            e.matmul(psRd[:, DS:2 * DS], aQhTs[:, i * C:(i + 1) * C],
                     aHH[:, i * 2 * DS + DS:(i + 1) * 2 * DS],
                     start=True, stop=False)
            return e.matmul(psRd[:, DS:2 * DS], aGqTs[:, i * C:(i + 1) * C],
                            aDb[:, i * 128 + DS:(i + 1) * 128],
                            start=False, stop=True)
        A("PE", mm_rd, reads=[f"aQhTf@{i}", f"aQhTs@{i}", f"aGqTf@{i}",
                              f"aGqTs@{i}", f"aDb@{i}", f"aHH@{i}"],
          writes=["psRd"])
        A("ACT", (lambda i_: lambda e: e.activation(
            aRbf[:, i_ * DS:(i_ + 1) * DS], psRd[:, 0:DS], AF.Copy))(i),
          reads=["psRd"], writes=[f"aRbf@{i}"])
        A("ACT", (lambda i_: lambda e: e.activation(
            aRbs[:, i_ * DS:(i_ + 1) * DS], psRd[:, DS:2 * DS], AF.Copy))(i),
          reads=["psRd"], writes=[f"aRbs@{i}"])
        r = i % R
        A("DVE", (lambda i_, r_: lambda e: e.scalar_tensor_tensor(
            rg["sc1"][r_], aRbf[:, i_ * DS:(i_ + 1) * DS], -0.5,
            aVtok[:, i_ * DS:(i_ + 1) * DS], ALU.mult, ALU.add))(i, r),
          reads=[f"aRbf@{i}", f"aVtok@{i}"], writes=[N(rg["sc1"][r])])
        A("DVE", (lambda i_, r_: lambda e: e.scalar_tensor_tensor(
            rg["sc2"][r_], aRbs[:, i_ * DS:(i_ + 1) * DS], -0.5,
            rg["sc1"][r_], ALU.mult, ALU.add))(i, r),
          reads=[f"aRbs@{i}", N(rg["sc1"][r])], writes=[N(rg["sc2"][r])])
        A("DVE", (lambda r_: lambda e: e.tensor_tensor(
            rg["sc1"][r_], rg["sc2"][r_], rg["sc2"][r_], ALU.mult))(r),
          reads=[N(rg["sc2"][r])], writes=[N(rg["sc1"][r])])
        A("DVE", (lambda r_: lambda e: e.reduce_sum(
            rg["errc"][r_], rg["sc1"][r_], AX.X))(r),
          reads=[N(rg["sc1"][r])], writes=[N(rg["errc"][r])])
        A("DVE", (lambda i_, r_: lambda e: e.tensor_scalar(
            aUnb[:, i_:i_ + 1], rg["errc"][r_], TH_ERR, None, ALU.is_gt))(i, r),
          reads=[N(rg["errc"][r])], writes=[f"aUnb@{i}"])

    _seccur[0] = 5
    A("SP", mk_spin(1), reads=["sRelay", "sFlagRx"], writes=["spun2", "sFlagRx"],
      tag="spin2")
    def emit_rx2(e):
        part = e.value_load(sRelay[0:1, 1:2])
        return e.dma_start(out=sRxNS,
                           in_=shState[bass.ts(part, 1), 128 * DS:129 * DS])
    A("SP", emit_rx2, reads=["spun2", "sRelay", "zs2"], writes=["sRxNS"], is_dma=True)
    A("DVE", lambda e: e.select(sRxNSs, sMask8[:, 0:1], sRxNS, sZeroHH[:, 0:1]),
      reads=["sRxNS", "sMask8", "sZeroHH"], writes=["sRxNSs"])
    A("DVE", lambda e: e.tensor_tensor(sFEu, sFlagRx, sNonceS, ALU.is_equal),
      reads=["sFlagRx", "sNonceS"], writes=["sFEu"])
    A("DVE", lambda e: e.tensor_copy(out=sFEb, in_=sFEu),
      reads=["sFEu"], writes=["sFEb"])
    A("PE", lambda e: e.matmul(psCh[:, 440:441], sONE1, sFEb,
                               start=True, stop=True),
      reads=["sONE1", "sFEb"], writes=["psCh"])
    A("DVE", lambda e: e.tensor_copy(out=sFEc, in_=psCh[:, 440:441]),
      reads=["psCh"], writes=["sFEc"])
    A("DVE", lambda e: e.tensor_scalar(sNS0, sRxNSs, sFEc[0:DS, :], None,
                                       ALU.mult),
      reads=["sRxNSs", "sFEc"], writes=["sNS0"])

    for i in range(NCH):
        r = i % R
        A("PE", (lambda i_: lambda e: e.matmul(
            psCh[0:1, 0:C], aUnb[:, i_:i_ + 1], sID,
            start=True, stop=True))(i),
          reads=[f"aUnb@{i}", "sID"], writes=["psCh"])
        A("DVE", lambda e: e.tensor_copy(out=rg["urow"][r],
                                         in_=psCh[0:1, 0:C]),
          reads=["psCh"], writes=[N(rg["urow"][r])])
        A("PE", lambda e: e.matmul(psCh[0:DS, 4 * DS:4 * DS + C],
                                   sONE1[:, 0:DS], rg["urow"][r],
                                   start=True, stop=True),
          reads=[N(rg["urow"][r]), "sONE1"], writes=["psCh"])
        A("DVE", lambda e: e.tensor_scalar(rg["cbc"][r],
                                           psCh[0:DS, 4 * DS:4 * DS + C],
                                           -0.1, 1.0, ALU.mult, ALU.add),
          reads=["psCh"], writes=[N(rg["cbc"][r])])
        A("DVE", (lambda i_: lambda e: e.scalar_tensor_tensor(
            rg["wns"][r], aNuT[:, i_ * C:(i_ + 1) * C], 0.1,
            psCh[0:DS, 4 * DS:4 * DS + C], ALU.mult, ALU.mult))(i),
          reads=[f"aNuT@{i}", "psCh"], writes=[N(rg["wns"][r])])
        A("DVE", lambda e: e.tensor_tensor_scan(
            rg["nsT"][r], rg["cbc"][r], rg["wns"][r], sNS0, ALU.mult, ALU.add),
          reads=[N(rg["cbc"][r]), N(rg["wns"][r]), "sNS0"],
          writes=[N(rg["nsT"][r])])
        A("DVE", lambda e: e.tensor_copy(out=sNS0, in_=rg["nsT"][r][:, C - 1:C]),
          reads=[N(rg["nsT"][r])], writes=["sNS0"])
        A("PE", lambda e: e.transpose(psCh[0:C, 6 * DS:7 * DS], rg["nsT"][r], sIDF[0:DS, 0:DS]),
          reads=[N(rg["nsT"][r]), "sIDF"], writes=["psCh"])
        A("DVE", (lambda i_: lambda e: e.tensor_copy(
            out=aNsTok[:, i_ * DS:(i_ + 1) * DS],
            in_=psCh[0:C, 6 * DS:7 * DS]))(i),
          reads=["psCh"], writes=[f"aNsTok@{i}"])

    A("DVE", lambda e: e.tensor_copy(out=sTxNS, in_=sNS0),
      reads=["sNS0"], writes=["sTxNS"])
    def emit_tx2(e):
        slot = e.value_load(sRelay[0:1, 0:1])
        return e.dma_start(out=shState[bass.ts(slot, 1), 128 * DS:129 * DS],
                           in_=sTxNS)
    A("SP", emit_tx2, reads=["sTxNS", "sRelay"], writes=["txd2"], is_dma=True)
    def emit_txf2(e):
        slot = e.value_load(sRelay[0:1, 0:1])
        return e.dma_start(out=shFlag[bass.ts(slot, 1), 1:2], in_=sNonceS)
    A("SP", emit_txf2, reads=["txd2", "sNonceS", "sRelay"], writes=["txf2"],
      is_dma=True)

    _seccur[0] = 6
    def emit_tailc(i):
        r = i % R
        cs = i * C
        xtl = rg["xtl"][r]
        A("SP", (lambda cs_, x_: lambda e: e.dma_start(
            out=x_.rearrange("p (g t) -> p g t", g=8),
            in_=dX_v[:, :, cs_:cs_ + C]))(cs, xtl),
          writes=[N(xtl)], is_dma=True)
        A("DVE", (lambda i_: lambda e: e.tensor_scalar(
            rg["sc1"][r], aDb[:, i_ * 128:i_ * 128 + DS], aQkf[:, i_:i_ + 1],
            None, ALU.mult))(i),
          reads=[f"aDb@{i}", f"aQkf@{i}"], writes=[N(rg["sc1"][r])])
        A("DVE", lambda e: e.tensor_scalar(rg["sc1"][r], rg["sc1"][r], sBF,
                                           None, ALU.mult),
          reads=["sBF"], writes=[N(rg["sc1"][r])])
        A("DVE", (lambda i_: lambda e: e.scalar_tensor_tensor(
            rg["rf"][r], aRbf[:, i_ * DS:(i_ + 1) * DS], sFA, rg["sc1"][r],
            ALU.mult, ALU.add))(i),
          reads=[f"aRbf@{i}", N(rg["sc1"][r]), "sFA"], writes=[N(rg["rf"][r])])
        A("DVE", (lambda i_: lambda e: e.tensor_scalar(
            rg["sc2"][r], aDb[:, i_ * 128 + DS:(i_ + 1) * 128],
            aQks[:, i_:i_ + 1], None, ALU.mult))(i),
          reads=[f"aDb@{i}", f"aQks@{i}"], writes=[N(rg["sc2"][r])])
        A("DVE", lambda e: e.tensor_scalar(rg["sc2"][r], rg["sc2"][r], sBS,
                                           None, ALU.mult),
          reads=["sBS"], writes=[N(rg["sc2"][r])])
        A("DVE", (lambda i_: lambda e: e.scalar_tensor_tensor(
            rg["rs"][r], aRbs[:, i_ * DS:(i_ + 1) * DS], sSA, rg["sc2"][r],
            ALU.mult, ALU.add))(i),
          reads=[f"aRbs@{i}", N(rg["sc2"][r]), "sSA"], writes=[N(rg["rs"][r])])
        pwt = aPwt[:, i * 4:i * 4 + 4]
        A("DVE", (lambda p_: lambda e: e.tensor_scalar(
            rg["o1"][r], rg["rf"][r], p_[:, 0:1], None, ALU.mult))(pwt),
          reads=[N(rg["rf"][r]), f"aPwt@{i}"], writes=[N(rg["o1"][r])])
        A("DVE", (lambda p_: lambda e: e.scalar_tensor_tensor(
            rg["o2"][r], rg["rs"][r], p_[:, 1:2], rg["o1"][r],
            ALU.mult, ALU.add))(pwt),
          reads=[N(rg["rs"][r]), f"aPwt@{i}", N(rg["o1"][r])],
          writes=[N(rg["o2"][r])])
        A("DVE", (lambda p_, i_: lambda e: e.scalar_tensor_tensor(
            rg["o1"][r], aNsTok[:, i_ * DS:(i_ + 1) * DS], p_[:, 2:3],
            rg["o2"][r], ALU.mult, ALU.add))(pwt, i),
          reads=[f"aNsTok@{i}", f"aPwt@{i}", N(rg["o2"][r])],
          writes=[N(rg["o1"][r])])
        A("ACT", lambda e: e.activation(rg["sc1"][r], rg["o1"][r], AF.Square,
                                        accum_out=rg["msc"][r]),
          reads=[N(rg["o1"][r])], writes=[N(rg["sc1"][r]), N(rg["msc"][r])])
        A("DVE", lambda e: e.tensor_scalar(rg["msc"][r], rg["msc"][r],
                                           1.0 / DS, 1e-6, ALU.mult, ALU.add),
          reads=[], writes=[N(rg["msc"][r])])
        A("ACT", lambda e: e.activation(rg["msc"][r], rg["msc"][r], AF.Ln),
          reads=[], writes=[N(rg["msc"][r])])
        A("ACT", lambda e: e.activation(rg["msc"][r], rg["msc"][r], AF.Exp,
                                        scale=-0.5),
          reads=[], writes=[N(rg["msc"][r])])
        A("DVE", lambda e: e.scalar_tensor_tensor(
            rg["ohat"][r], rg["o1"][r], rg["msc"][r], sNW, ALU.mult, ALU.mult),
          reads=[N(rg["o1"][r]), N(rg["msc"][r]), "sNW"],
          writes=[N(rg["ohat"][r])])
        A("PE", lambda e: e.transpose(psTr[0:DS, 3 * C:4 * C],
                                      rg["ohat"][r], sID),
          reads=[N(rg["ohat"][r]), "sID"], writes=["psTr"])
        A("DVE", lambda e: e.tensor_copy(out=rg["ohT"][r],
                                         in_=psTr[0:DS, 3 * C:4 * C]),
          reads=["psTr"], writes=[N(rg["ohT"][r])])
        for hf in range(2):
            def mm_wo(e, hf=hf, r=r):
                last = None
                for g in range(4):
                    gg = hf * 4 + g
                    last = e.matmul(psOut[:, g * C:(g + 1) * C],
                                    sWoT[:, gg * C:(gg + 1) * C], rg["ohT"][r],
                                    start=True, stop=True)
                return last
            A("PE", mm_wo, reads=[N(rg["ohT"][r]), "sWoT"], writes=["psOut"])
            xvh = xtl.rearrange("p (g t) -> p g t", g=8)[:, hf * 4:(hf + 1) * 4, :]
            A("DVE", (lambda xv_: lambda e: e.tensor_tensor(
                xv_, xv_, psOut.rearrange("p (g t) -> p g t", g=4), ALU.add))(xvh),
              reads=["psOut"], writes=[N(xtl)])
        op = A("SP", (lambda cs_, x_: lambda e: e.dma_start(
            out=dX1_v[:, :, cs_:cs_ + C],
            in_=x_.rearrange("p (g t) -> p g t", g=8)))(cs, xtl),
            reads=[N(xtl)], is_dma=True)
        out_dmas.append(op)
        xhat = emit_ln(xtl, N(xtl), r, i, save_stats=True)
        def mm_rt(e):
            last = None
            for g in range(8):
                last = e.matmul(psTok[:, 200:208], xhat[:, g * C:(g + 1) * C],
                                sGWT[:, g * 8:(g + 1) * 8],
                                start=(g == 0), stop=(g == 7))
            return last
        A("PE", mm_rt, reads=[N(xhat), "sGWT"], writes=["psTok"])
        scr, a8, b8 = rg["scr"][r], rg["sc8a"][r], rg["sc8b"][r]
        c1, c2, c3 = rg["c1"][r], rg["c2"][r], rg["c3"][r]
        A("DVE", lambda e: e.tensor_tensor(scr, psTok[:, 200:208], sLB, ALU.add),
          reads=["psTok", "sLB"], writes=[N(scr)])
        A("ACT", lambda e: e.activation(scr, scr, AF.Sigmoid),
          reads=[], writes=[N(scr)])
        A("DVE", lambda e: e.reduce_max(c1, scr, AX.X),
          reads=[N(scr)], writes=[N(c1)])
        A("DVE", lambda e: e.tensor_scalar(a8, scr, c1, None, ALU.is_ge),
          reads=[N(scr), N(c1)], writes=[N(a8)])
        A("DVE", lambda e: e.scalar_tensor_tensor(a8, a8, -BIG, sIOTA8B,
                                                  ALU.mult, ALU.add),
          reads=["sIOTA8B"], writes=[N(a8)])
        A("DVE", lambda e: e.tensor_reduce(c2, a8, AX.X, ALU.min),
          reads=[N(a8)], writes=[N(c2)])
        A("DVE", lambda e: e.tensor_scalar(a8, sIOTA8, c2, None, ALU.is_equal),
          reads=[N(c2), "sIOTA8"], writes=[N(a8)])
        A("DVE", lambda e: e.scalar_tensor_tensor(b8, a8, -BIG, scr,
                                                  ALU.mult, ALU.add),
          reads=[N(a8), N(scr)], writes=[N(b8)])
        A("DVE", lambda e: e.reduce_max(c2, b8, AX.X),
          reads=[N(b8)], writes=[N(c2)])
        A("DVE", lambda e: e.tensor_scalar(b8, b8, c2, None, ALU.is_ge),
          reads=[N(c2)], writes=[N(b8)])
        A("DVE", lambda e: e.scalar_tensor_tensor(b8, b8, -BIG, sIOTA8B,
                                                  ALU.mult, ALU.add),
          reads=["sIOTA8B"], writes=[N(b8)])
        A("DVE", lambda e: e.tensor_reduce(c3, b8, AX.X, ALU.min),
          reads=[N(b8)], writes=[N(c3)])
        A("DVE", lambda e: e.tensor_scalar(b8, sIOTA8, c3, None, ALU.is_equal),
          reads=[N(c3), "sIOTA8"], writes=[N(b8)])
        A("DVE", lambda e: e.tensor_tensor(c3, c1, c2, ALU.add),
          reads=[N(c1), N(c2)], writes=[N(c3)])
        A("DVE", lambda e: e.tensor_scalar(c3, c3, 1e-8, None, ALU.add),
          reads=[], writes=[N(c3)])
        A("DVE", lambda e: e.reciprocal(c3, c3), reads=[], writes=[N(c3)])
        A("DVE", lambda e: e.tensor_scalar(c1, c1, c3, None, ALU.mult),
          reads=[N(c3)], writes=[N(c1)])
        A("DVE", lambda e: e.tensor_scalar(c2, c2, c3, None, ALU.mult),
          reads=[N(c3)], writes=[N(c2)])
        A("DVE", lambda e: e.tensor_tensor(rg["ohs"][r], a8, b8, ALU.add),
          reads=[N(a8), N(b8)], writes=[N(rg["ohs"][r])])
        A("DVE", lambda e: e.tensor_scalar(b8, b8, c2, None, ALU.mult),
          reads=[N(c2)], writes=[N(b8)])
        A("DVE", lambda e: e.scalar_tensor_tensor(rg["ohw"][r], a8, c1, b8,
                                                  ALU.mult, ALU.add),
          reads=[N(a8), N(c1), N(b8)], writes=[N(rg["ohw"][r])])
        A("DVE", (lambda i_: lambda e: e.tensor_copy(
            out=sOHSG[:, (i_ % 4) * 8:(i_ % 4 + 1) * 8], in_=rg["ohs"][r]))(i),
          reads=[N(rg["ohs"][r])], writes=[f"sOHSG@{i % 4}"])
        def mm_part(e, i=i, r=r):
            e.matmul(psTok[0:1, 208:216], sONE128, rg["ohw"][r],
                     start=True, stop=True)
            return e.matmul(psTok[0:1, 216:224], sONE128,
                            sOHSG[:, (i % 4) * 8:(i % 4 + 1) * 8],
                            start=True, stop=True)
        A("PE", mm_part, reads=[N(rg["ohw"][r]), f"sOHSG@{i % 4}", "sONE128"],
          writes=["psTok"])
        A("DVE", lambda e: e.tensor_tensor(sPart, sPart, psTok[0:1, 208:224],
                                           ALU.add),
          reads=["psTok"], writes=["sPart"])
        if i % 4 == 3:
            g4 = i // 4
            A("PE", lambda e: e.transpose(psTr[0:32, 2 * C:3 * C], sOHSG, sID),
              reads=["sOHSG@0", "sOHSG@1", "sOHSG@2", "sOHSG@3", "sID"],
              writes=["psTr"])
            A("DVE", lambda e: e.tensor_copy(out=sSelT,
                                             in_=psTr[0:32, 2 * C:3 * C]),
              reads=["psTr"], writes=["sSelT"])
            op = A("SP", (lambda g_: lambda e: e.dma_start(
                out=dSELS_v(g_), in_=sSelT))(g4),
                reads=["sSelT"], is_dma=True)
            out_dmas.append(op)

    for i in range(NCH):
        emit_tailc(i)

    op = A("SP", lambda e: e.dma_start(out=oPART, in_=sPart), reads=["sPart"],
           is_dma=True)
    out_dmas.append(op)
    if _dbgarr:
        for key, arr in [("dbgD0", aD0), ("dbgMhT", aMhT), ("dbgVtok", aVtok),
                         ("dbgQhTf", aQhTf), ("dbgNuT", aNuT), ("dbgDb", aDb),
                         ("dbgRbf", aRbf), ("dbgNsTok", aNsTok),
                         ("dbgKpf", aKpf), ("dbgPwt", aPwt)]:
            rd = ["a" + key[3:] + "@" + str(j) for j in range(NCH)]
            op = A("SP", (lambda k_, a_: lambda e: e.dma_start(
                out=oDBG[k_], in_=a_))(key, arr), reads=rd, is_dma=True)
            out_dmas.append(op)
    _seccur[0] = 1
    A("SP", lambda e: e.nop(), dep_on=[o for o in out_dmas if o is not None])
    if os.environ.get("MK_PRCNT"):
        print("opcount:", _ocnt[0])
    s.run()
    ctx.pop_all()
    return nc


def build_ffn():
    nc = bass.Bass("TRN2", target_bir_lowering=False, detect_race_conditions=False)
    ctx = contextlib.ExitStack()

    def din(name, shape, dtype=F32):
        return nc.dram_tensor(name, shape, dtype, kind="ExternalInput").ap()

    def sbuf(name, shape, dtype=F32):
        return ctx.enter_context(nc.sbuf_tensor(name, shape, dtype)).ap()

    def psum(name, shape, dtype=F32):
        return ctx.enter_context(nc.psum_tensor(name, shape, dtype)).ap()

    dX1 = din("x1T", [D, TOK])
    dSTATS = din("stats", [2, TOK])
    dSELS = din("sels", [8, TOK], BF16)
    dPART = din("allpart", [8, 16])
    dWG = din("wgT", [10, D, GRP])
    dWU = din("wuT", [10, D, GRP])
    dWD = din("wdT", [10, GRP, D])

    oOut = nc.dram_tensor("outT", [D, TOK], F32, kind="ExternalOutput").ap()
    dX1_v = dX1.rearrange("(g p) t -> p g t", p=128)
    dOut_v = oOut.rearrange("(g p) t -> p g t", p=128)

    scrWG = nc.dram_tensor("scrWG", [10, D, GRP], BF16, kind="Internal").ap()
    scrWU = nc.dram_tensor("scrWU", [10, D, GRP], BF16, kind="Internal").ap()
    scrWD = nc.dram_tensor("scrWD", [10, GRP, D], BF16, kind="Internal").ap()
    wmScr = nc.dram_tensor("wmScr", [1, 16], F32, kind="Internal").ap()

    x_q = sbuf("x_q", [128, 8 * QTOK])
    xhat = sbuf("xhat", [128, 8 * QTOK], BF16)
    slWG = [sbuf(f"slWG{j}", [128, 8 * GRP], BF16) for j in range(2)]
    slWU = [sbuf(f"slWU{j}", [128, 8 * GRP], BF16) for j in range(2)]
    slWD = [sbuf(f"slWD{j}", [128, 4 * D], BF16) for j in range(2)]
    sSilu = sbuf("sSilu", [128, 4 * GRP], BF16)
    sT1 = sbuf("sT1", [128, 4 * GRP], BF16)
    sSel = sbuf("sSel", [128, GRP], BF16)
    sDn = sbuf("sDn", [128, 4 * GRP], BF16)
    sMbc = sbuf("sMbc", [128, GRP])
    sRbc = sbuf("sRbc", [128, GRP])
    sLnT = sbuf("sLnT", [128, 8 * GRP])
    sPartA = sbuf("sPartA", [8, 16])
    sOnes8 = sbuf("sOnes8", [8, 1])
    sWm = sbuf("sWm", [1, 16])
    sWmbc = sbuf("sWmbc", [128, 16])

    psG = psum("psG", [128, 2048])
    psU = psum("psU", [128, 2048])

    s = Sched(nc)
    A = s.add
    out_dmas = []

    A("SP", lambda e: e.dma_start(out=sPartA, in_=dPART), writes=["sPartA"], is_dma=True)
    A("DVE", lambda e: e.memset(sOnes8, 1.0), writes=["sOnes8"])
    A("PE", lambda e: e.matmul(psG[0:1, 0:16], sOnes8, sPartA, start=True, stop=True),
      reads=["sPartA", "sOnes8"], writes=["psG"])
    A("DVE", lambda e: e.tensor_copy(out=sWm, in_=psG[0:1, 0:16]),
      reads=["psG"], writes=["sWm"])
    A("DVE", lambda e: e.tensor_scalar(sWm[:, 8:16], sWm[:, 8:16], 1.0, None,
                                       ALU.max), reads=[], writes=["sWm"])
    A("DVE", lambda e: e.reciprocal(sWm[:, 8:16], sWm[:, 8:16]),
      reads=[], writes=["sWm"])
    A("DVE", lambda e: e.tensor_tensor(sWm[:, 0:8], sWm[:, 0:8], sWm[:, 8:16],
                                       ALU.mult), reads=[], writes=["sWm"])
    A("SP", lambda e: e.dma_start(out=wmScr, in_=sWm), reads=["sWm"],
      writes=["wmScr"], is_dma=True)
    A("SP", lambda e: e.dma_start(out=sWmbc, in_=_bpar(wmScr)),
      reads=["wmScr"], writes=["sWmbc"], is_dma=True)

    for e_i in range(10):
        for kind, src, dst in [("g", dWG, scrWG), ("u", dWU, scrWU),
                               ("d", dWD, scrWD)]:
            A("POOL", (lambda s_, d_, ei_: lambda e: e.dma_start(
                out=d_[ei_], in_=s_[ei_]))(src, dst, e_i),
              writes=[f"cast{kind}{e_i}"], is_dma=True)

    for p_i in range(NPASS):
        q0 = p_i * QTOK
        A("SP", (lambda q0_: lambda e: e.dma_start(
            out=x_q.rearrange("p (g t) -> p g t", g=8),
            in_=dX1_v[:, :, q0_:q0_ + QTOK]))(q0),
          writes=["x_q"], is_dma=True)
        for gq in range(GPP):
            g = p_i * GPP + gq
            cs = g * GRP
            xv = x_q.rearrange("p (g t) -> p g t", g=8)[
                :, :, gq * GRP:(gq + 1) * GRP]
            A("SP", (lambda cs_: lambda e: e.dma_start(
                out=sMbc, in_=_bpar(dSTATS[0:1, cs_:cs_ + GRP])))(cs),
              writes=["sMbc"], is_dma=True)
            A("SP", (lambda cs_: lambda e: e.dma_start(
                out=sRbc, in_=_bpar(dSTATS[1:2, cs_:cs_ + GRP])))(cs),
              writes=["sRbc"], is_dma=True)
            xho = xhat.rearrange("p (g t) -> p g t", g=8)[
                :, :, gq * GRP:(gq + 1) * GRP]
            A("DVE", (lambda xv_: lambda e: e.tensor_tensor(
                sLnT.rearrange("p (g t) -> p g t", g=8), xv_,
                _rep(sMbc, 8), ALU.subtract))(xv),
              reads=["x_q", "sMbc"], writes=["sLnT"])
            A("DVE", (lambda xho_: lambda e: e.tensor_tensor(
                xho_, sLnT.rearrange("p (g t) -> p g t", g=8),
                _rep(sRbc, 8), ALU.mult))(xho),
              reads=["sLnT", "sRbc"], writes=[f"xhat@{gq}"])
        for e_i in range(10):
            sl = e_i % 2
            A("SP", (lambda ei_, sl_: lambda e: e.dma_start(
                out=slWG[sl_],
                in_=scrWG[ei_].rearrange("(g p) i -> p g i", p=128)))(e_i, sl),
              reads=[f"castg{e_i}"], writes=[f"slWG{sl}"], is_dma=True)
            A("SP", (lambda ei_, sl_: lambda e: e.dma_start(
                out=slWU[sl_],
                in_=scrWU[ei_].rearrange("(g p) i -> p g i", p=128)))(e_i, sl),
              reads=[f"castu{e_i}"], writes=[f"slWU{sl}"], is_dma=True)
            A("SP", (lambda ei_, sl_: lambda e: e.dma_start(
                out=slWD[sl_],
                in_=scrWD[ei_].rearrange("(c p) h -> p c h", p=128)))(e_i, sl),
              reads=[f"castd{e_i}"], writes=[f"slWD{sl}"], is_dma=True)
            for gq in range(GPP):
                g = p_i * GPP + gq
                cs = g * GRP
                if e_i >= 2:
                    A("SP", (lambda ei_, cs_: lambda e: e.dma_start(
                        out=sSel, in_=_bpar(dSELS[ei_ - 2:ei_ - 1,
                                                  cs_:cs_ + GRP])))(e_i, cs),
                      writes=["sSel"], is_dma=True)
                def mm_gate(e, sl=sl, gq=gq):
                    last = None
                    for it in range(4):
                        for g8 in range(8):
                            last = e.matmul(
                                psG[:, it * GRP:(it + 1) * GRP],
                                slWG[sl][:, g8 * GRP + it * 128:
                                         g8 * GRP + (it + 1) * 128],
                                xhat[:, g8 * QTOK + gq * GRP:
                                     g8 * QTOK + (gq + 1) * GRP],
                                start=(g8 == 0), stop=(g8 == 7))
                    return last
                A("PE", mm_gate, reads=[f"slWG{sl}", f"xhat@{gq}"], writes=["psG"])
                A("ACT", lambda e: e.activation(sSilu, psG, AF.Silu),
                  reads=["psG"], writes=["sSilu"])
                def mm_up(e, sl=sl, gq=gq):
                    last = None
                    for it in range(4):
                        for g8 in range(8):
                            last = e.matmul(
                                psU[:, it * GRP:(it + 1) * GRP],
                                slWU[sl][:, g8 * GRP + it * 128:
                                         g8 * GRP + (it + 1) * 128],
                                xhat[:, g8 * QTOK + gq * GRP:
                                     g8 * QTOK + (gq + 1) * GRP],
                                start=(g8 == 0), stop=(g8 == 7))
                    return last
                A("PE", mm_up, reads=[f"slWU{sl}", f"xhat@{gq}"], writes=["psU"])
                A("DVE", lambda e: e.tensor_tensor(sT1, psU, sSilu, ALU.mult),
                  reads=["psU", "sSilu"], writes=["sT1"])
                if e_i >= 2:
                    A("DVE", (lambda ei_: lambda e: e.scalar_tensor_tensor(
                        sT1.rearrange("p (i t) -> p i t", i=4),
                        sT1.rearrange("p (i t) -> p i t", i=4),
                        sWmbc[:, ei_ - 2:ei_ - 1], _rep(sSel, 4),
                        ALU.mult, ALU.mult))(e_i),
                      reads=["sWmbc", "sSel"], writes=["sT1"])
                for hf in range(2):
                    ps = psG if hf == 0 else psU
                    pk = "psG" if hf == 0 else "psU"
                    def mm_dn(e, sl=sl, hf=hf, ps=ps):
                        last = None
                        for ht in range(4):
                            gg = hf * 4 + ht
                            for ic in range(4):
                                last = e.matmul(
                                    ps[:, ht * GRP:(ht + 1) * GRP],
                                    slWD[sl][:, ic * D + gg * 128:
                                             ic * D + (gg + 1) * 128],
                                    sT1[:, ic * GRP:(ic + 1) * GRP],
                                    start=(ic == 0), stop=(ic == 3))
                        return last
                    A("PE", mm_dn, reads=[f"slWD{sl}", "sT1"], writes=[pk])
                    xo = x_q.rearrange("p (g t) -> p g t", g=8)[
                        :, hf * 4:(hf + 1) * 4, gq * GRP:(gq + 1) * GRP]
                    if hf == 0:
                        A("DVE", (lambda xo_, ps_: lambda e: e.tensor_tensor(
                            xo_, xo_, ps_.rearrange("p (g t) -> p g t", g=4),
                            ALU.add))(xo, ps),
                          reads=[pk], writes=["x_q"])
                    else:
                        A("ACT", (lambda ps_: lambda e: e.activation(
                            sDn, ps_, AF.Copy))(ps),
                          reads=[pk], writes=["sDn"])
                        A("POOL", (lambda xo_: lambda e: e.tensor_tensor(
                            xo_, xo_, sDn.rearrange("p (g t) -> p g t", g=4),
                            ALU.add))(xo),
                          reads=["sDn"], writes=["x_q"])
        op = A("SP", (lambda q0_: lambda e: e.dma_start(
            out=dOut_v[:, :, q0_:q0_ + QTOK],
            in_=x_q.rearrange("p (g t) -> p g t", g=8)))(q0),
            reads=["x_q"], is_dma=True)
        out_dmas.append(op)
    A("SP", lambda e: e.nop(), dep_on=out_dmas)
    s.run()
    ctx.pop_all()
    return nc


_CACHE = {}


def _consts():
    t = np.arange(128, dtype=np.float32)
    TT, SS = np.meshgrid(t, t, indexing="ij")
    exTS = (TT - 1 - SS).astype(np.float32)
    return dict(
        identb=np.eye(128, dtype=nbf),
        exTS=exTS,
        exST=np.ascontiguousarray(exTS.T),
        mskSL=(SS < TT).astype(np.float32),
        mskSU=np.ascontiguousarray((SS < TT).astype(np.float32).T),
        iotaRow=np.ascontiguousarray(t.reshape(1, 128)),
        iotaCol=np.ascontiguousarray(t.reshape(128, 1)),
        iotaColRev=np.ascontiguousarray((127.0 - t).reshape(128, 1)),
    )


def kernel(L0_attn_x, L0_attn_Wk, L0_attn_Wv, L0_attn_Wq, L0_attn_fast_alpha,
           L0_attn_slow_alpha, L0_attn_nm_w1, L0_attn_nm_b1, L0_attn_nm_w2,
           L0_attn_nm_b2, L0_attn_ir_w1, L0_attn_ir_b1, L0_attn_ir_w2,
           L0_attn_ir_b2, L0_attn_pw, L0_attn_Wo, L0_attn_norm_w,
           L0_ffn_gate_w, L0_ffn_lb_bias, L0_ffn_expert_gate_w,
           L0_ffn_expert_up_w, L0_ffn_expert_down_w, L0_ffn_shared_gate_w,
           L0_ffn_shared_up_w, L0_ffn_shared_down_w):
    f = np.float32
    asn = np.ascontiguousarray
    x = np.asarray(L0_attn_x, f)
    consts = _consts()

    if "attn" not in _CACHE:
        _CACHE["attn"] = build_attn()
    nc1 = _CACHE["attn"]

    common1 = dict(
        WkT=asn(np.asarray(L0_attn_Wk, f).T), WvT=asn(np.asarray(L0_attn_Wv, f).T),
        WqT=asn(np.asarray(L0_attn_Wq, f).T), ir1T=asn(np.asarray(L0_attn_ir_w1, f).T),
        ir2T=asn(np.asarray(L0_attn_ir_w2, f).T), pwT=asn(np.asarray(L0_attn_pw, f).T),
        nm1T=asn(np.asarray(L0_attn_nm_w1, f).T), nm2T=asn(np.asarray(L0_attn_nm_w2, f).T),
        nmb1=np.asarray(L0_attn_nm_b1, f).reshape(-1, 1),
        nmb2=np.asarray(L0_attn_nm_b2, f).reshape(-1, 1),
        irb1=np.asarray(L0_attn_ir_b1, f).reshape(-1, 1),
        irb2=np.asarray(L0_attn_ir_b2, f).reshape(1, 1),
        WoT=asn(np.asarray(L0_attn_Wo, f).T),
        normw=np.asarray(L0_attn_norm_w, f).reshape(1, -1),
        gwT=asn(np.asarray(L0_ffn_gate_w, f).T),
        lb=np.asarray(L0_ffn_lb_bias, f).reshape(1, -1),
        fa=np.asarray(L0_attn_fast_alpha, f).reshape(1, 1),
        sa=np.asarray(L0_attn_slow_alpha, f).reshape(1, 1),
        **consts,
    )
    nonce = _fresh_nonce()
    in_maps1 = []
    for c in range(8):
        b, h = c // 2, c % 2
        m = dict(common1)
        m["xT"] = asn(x[b, h * TOK:(h + 1) * TOK, :].T)
        m["relay"] = np.array([[h, (1 - h) if h == 1 else h,
                                1 if h == 1 else 0, nonce + b]], np.int32)
        m["isodd"] = np.array([[1.0 if h == 1 else 0.0]], f)
        in_maps1.append(m)
    res1 = run_bass_kernel_spmd(nc1, in_maps1, core_ids=list(range(8)))
    r1 = res1.results

    if "ffn" not in _CACHE:
        _CACHE["ffn"] = build_ffn()
    nc2 = _CACHE["ffn"]
    allpart = np.concatenate([np.asarray(r1[c]["partials"], f) for c in range(8)],
                             axis=0)
    sgT = np.asarray(L0_ffn_shared_gate_w, f).T
    suT = np.asarray(L0_ffn_shared_up_w, f).T
    sdT = np.asarray(L0_ffn_shared_down_w, f).T
    eg = np.asarray(L0_ffn_expert_gate_w, f)
    eu = np.asarray(L0_ffn_expert_up_w, f)
    ed = np.asarray(L0_ffn_expert_down_w, f)
    wg = np.empty((10, D, GRP), f); wu = np.empty((10, D, GRP), f)
    wd = np.empty((10, GRP, D), f)
    wg[0] = sgT[:, 0:GRP]; wg[1] = sgT[:, GRP:2 * GRP]
    wu[0] = suT[:, 0:GRP]; wu[1] = suT[:, GRP:2 * GRP]
    wd[0] = sdT[0:GRP, :]; wd[1] = sdT[GRP:2 * GRP, :]
    for e_i in range(8):
        wg[2 + e_i] = eg[e_i].T
        wu[2 + e_i] = eu[e_i].T
        wd[2 + e_i] = ed[e_i].T

    common2 = dict(allpart=allpart, wgT=wg, wuT=wu, wdT=wd)
    in_maps2 = []
    for c in range(8):
        m = dict(common2)
        m["x1T"] = np.asarray(r1[c]["x1T"], f)
        m["stats"] = np.asarray(r1[c]["stats"], f)
        m["sels"] = np.asarray(r1[c]["sels"]).astype(nbf)
        in_maps2.append(m)
    res2 = run_bass_kernel_spmd(nc2, in_maps2, core_ids=list(range(8)))
    r2 = res2.results

    out = np.empty((4, 8192, D), f)
    for c in range(8):
        b, h = c // 2, c % 2
        out[b, h * TOK:(h + 1) * TOK, :] = np.asarray(r2[c]["outT"], f).T
    return out
